# version 26
# baseline (speedup 1.0000x reference)
import numpy as np
import ml_dtypes
from contextlib import ExitStack

import concourse.bass as bass
import concourse.mybir as mybir
from concourse.bass_utils import run_bass_kernel_spmd

F32 = mybir.dt.float32
BF16 = mybir.dt.bfloat16
I32 = mybir.dt.int32
AF = mybir.ActivationFunctionType
ALU = mybir.AluOpType
AXX = mybir.AxisListType.X

SEQS = 2
T = 2048
D = 1024
NT = 16
NCH = 8
NSP = 4
SPW = 512
H = 16
EXP_BIAS = -12.0
EPS = 1e-5
INVD = 1.0 / D
MAGIC_F = float(np.frombuffer(np.uint32(0x5F3759DF).tobytes(), np.float32)[0])

WNAMES = ["wq", "wk", "wv", "wo", "wr", "wkf", "wvf"]


def _sel_const():
    sel = np.zeros((128, 64 * 64), dtype=ml_dtypes.bfloat16)
    for p in range(64):
        sel[p, 64 * p:64 * p + 64] = 1
    return sel


class Ctr:

    def __init__(self, ctx, nc, name):
        self.h = ctx.enter_context(nc.semaphore(name))
        self.v = 0

    def inc(self, instr, by=1):
        instr.then_inc(self.h, by)
        self.v += by
        return self.v

    def dma(self, instr):
        return self.inc(instr, 16)


def build_nc():
    nc = bass.Bass("TRN2", target_bir_lowering=False)
    TE, VE, SE, SY, GP = nc.tensor, nc.vector, nc.scalar, nc.sync, nc.gpsimd

    x_h = nc.dram_tensor("x", [SEQS, T, D], F32, kind="ExternalInput").ap()
    w_h = {n: nc.dram_tensor(n, [D, D], F32, kind="ExternalInput").ap()
           for n in WNAMES}
    mur_h = nc.dram_tensor("mur", [128, 8], F32, kind="ExternalInput").ap()
    muk_h = nc.dram_tensor("muk", [128, 8], F32, kind="ExternalInput").ap()
    idf_h = nc.dram_tensor("idf", [128, 128], F32, kind="ExternalInput").ap()
    tri_h = nc.dram_tensor("tri", [128, 128], BF16, kind="ExternalInput").ap()
    sel_h = nc.dram_tensor("sel", [128, 64 * 64], BF16, kind="ExternalInput").ap()
    y_h = nc.dram_tensor("y", [SEQS, T, D], F32, kind="ExternalOutput").ap()
    wscr = nc.dram_tensor("wscr", [7, 128, 8 * D], BF16, kind="Internal").ap()

    CTX = ExitStack()
    with CTX:
        idf = CTX.enter_context(nc.sbuf_tensor("idf_s", [128, 128], F32))

        st = Ctr(CTX, nc, "st")
        ss = Ctr(CTX, nc, "ss")
        sv = Ctr(CTX, nc, "sv")
        cst = Ctr(CTX, nc, "cst")
        xa = [Ctr(CTX, nc, f"xa{i}") for i in range(2)]
        xg = [Ctr(CTX, nc, f"xg{i}") for i in range(2)]
        ysem = [Ctr(CTX, nc, f"y{i}") for i in range(4)]
        dst_s = [Ctr(CTX, nc, f"dsx{i}") for i in range(2)]
        ost_s = [Ctr(CTX, nc, f"osx{i}") for i in range(2)]
        wst_s = [Ctr(CTX, nc, f"wsx{i}") for i in range(2)]
        wsl_s = [Ctr(CTX, nc, f"wl{i}") for i in range(4)]
        wscr_s = Ctr(CTX, nc, "wsc")

        def vop(instr):
            v = sv.inc(instr)
            VE.wait_ge(sv.h, v)
            return v

        cst.dma(SY.dma_start(out=idf[:], in_=idf_h))
        consts_loaded = cst.v

        scrdone = {}
        with ExitStack() as prep:
            wstage = [prep.enter_context(nc.sbuf_tensor(f"wst{i}", [128, D], F32))
                      for i in range(2)]
            asm = prep.enter_context(nc.sbuf_tensor("wasm", [128, 8 * D], BF16))
            asm_r = asm[:].rearrange("p (c w) -> p c w", w=D)
            pp = [prep.enter_context(nc.psum_tensor(f"pp{i}", [128, 512], F32))
                  for i in range(4)]
            warm = prep.enter_context(nc.sbuf_tensor("warm", [128, 512], F32))
            tpdone = {}
            evdone = {}
            WT = 8
            SE.wait_ge(cst.h, consts_loaded)
            for _ in range(8):
                wv = ss.inc(SE.activation(out=warm[:, 0:128], in_=idf[:],
                                          func=AF.Exp))
                SE.wait_ge(ss.h, wv)
            for wi, wn in enumerate(WNAMES):
                for t in range(WT):
                    if t >= 2:
                        GP.wait_ge(st.h, tpdone[(wi, t - 2)])
                    elif wi > 0:
                        GP.wait_ge(st.h, tpdone[(wi - 1, WT - 2 + t)])
                    ld = wst_s[t % 2].dma(GP.dma_start(
                        out=wstage[t % 2][:],
                        in_=w_h[wn][128 * t:128 * t + 128, :]))
                    TE.wait_ge(wst_s[t % 2].h, ld)
                    if wi == 0 and t == 0:
                        TE.wait_ge(cst.h, consts_loaded)
                    if t >= 2:
                        TE.wait_ge(sv.h, evdone[(wi, t - 2)])
                    elif wi > 0:
                        TE.wait_ge(sv.h, evdone[(wi - 1, WT - 2 + t)])
                    for b in range(8):
                        ti = TE.transpose(
                            out=pp[2 * (t % 2) + b // 4][:, 128 * (b % 4):128 * (b % 4) + 128],
                            in_=wstage[t % 2][:, 128 * b:128 * b + 128],
                            identity=idf[:])
                    tpdone[(wi, t)] = st.inc(ti)
                    VE.wait_ge(st.h, tpdone[(wi, t)])
                    if t == 0 and wi > 0:
                        VE.wait_ge(wscr_s.h, scrdone[wi - 1])
                    vop(VE.tensor_copy(out=asm_r[:, 0:4, 128 * t:128 * t + 128],
                                       in_=pp[2 * (t % 2)][:].rearrange("p (c w) -> p c w", w=128)))
                    ev = VE.tensor_copy(out=asm_r[:, 4:8, 128 * t:128 * t + 128],
                                        in_=pp[2 * (t % 2) + 1][:].rearrange("p (c w) -> p c w", w=128))
                    evdone[(wi, t)] = vop(ev)
                GP.wait_ge(sv.h, evdone[(wi, WT - 1)])
                scrdone[wi] = wscr_s.dma(GP.dma_start(out=wscr[wi], in_=asm[:]))
            prep_end = dict(st=st.v, sv=sv.v, ss=ss.v)

        seq_end = None
        ydma_prev = None
        for s in range(SEQS):
            with ExitStack() as seqctx:
                O = seqctx.enter_context(
                    nc.sbuf_tensor(f"O{s}", [128, NCH * T], BF16))
                with ExitStack() as qs:
                    Q = qs.enter_context(nc.sbuf_tensor(f"Q{s}", [128, NCH * T], BF16))
                    K = qs.enter_context(nc.sbuf_tensor(f"K{s}", [128, NCH * T], BF16))
                    Vp = qs.enter_context(nc.sbuf_tensor(f"V{s}", [128, NT * 1040], BF16))
                    if s == 0:
                        VE.wait_ge(st.h, prep_end["st"])
                        VE.wait_ge(ss.h, prep_end["ss"])
                        VE.wait_ge(wscr_s.h, scrdone[len(WNAMES) - 1])
                    else:
                        VE.wait_ge(st.h, seq_end["st"])
                        VE.wait_ge(ss.h, seq_end["ss"])
                        for yy in ysem:
                            VE.wait_ge(yy.h, yy.v)
                    vop(VE.memset(
                        Vp[:].rearrange("p (i g w) -> p i g w", g=16, w=65)[:, :, :, 64:65],
                        1.0))
                    vones0 = vop(VE.memset(
                        Vp[:].rearrange("p (i g w) -> p i g w", g=16, w=65)[:, :, :, 64:65],
                        1.0))

                    with ExitStack() as pa:
                        slots = [pa.enter_context(
                            nc.sbuf_tensor(f"sl{s}{i}", [128, 8 * D], BF16))
                            for i in range(3)]
                        xstage = [pa.enter_context(
                            nc.sbuf_tensor(f"xs{s}{i}", [128, D], F32)) for i in range(2)]
                        xfeat = pa.enter_context(
                            nc.sbuf_tensor(f"xf{s}", [128, NCH * SPW], BF16))
                        xfeat_r = xfeat[:].rearrange("p (c w) -> p c w", w=SPW)
                        tpa = [pa.enter_context(
                            nc.psum_tensor(f"tp{s}{i}", [128, 512], F32)) for i in range(4)]
                        qk_ps = [pa.enter_context(
                            nc.psum_tensor(f"qk{s}{i}", [128, 512], F32)) for i in range(2)]
                        v_ps = pa.enter_context(nc.psum_tensor(f"vv{s}", [128, 1024], F32))

                        if s == 0:
                            GP.wait_ge(st.h, prep_end["st"])
                            GP.wait_ge(sv.h, prep_end["sv"])
                            GP.wait_ge(wscr_s.h, scrdone[2])
                        else:
                            GP.wait_ge(st.h, seq_end["st"])
                            GP.wait_ge(sv.h, seq_end["sv"])
                            GP.wait_ge(ss.h, seq_end["ss"])
                            SY.wait_ge(st.h, seq_end["st"])
                            SY.wait_ge(sv.h, seq_end["sv"])
                            SY.wait_ge(ss.h, seq_end["ss"])
                        slotld = [wsl_s[i].dma(GP.dma_start(out=slots[i][:],
                                                             in_=wscr[i]))
                                  for i in range(3)]

                        tpadone = {}
                        xfdone = {}
                        evQK = {}
                        evV = {}
                        for sp in range(NSP):
                            for u in range(4):
                                i = 4 * sp + u
                                if i >= 2:
                                    SY.wait_ge(st.h, tpadone[i - 2])
                                ld = xa[i % 2].dma(SY.dma_start(
                                    out=xstage[i % 2][:],
                                    in_=x_h[s, 128 * i:128 * i + 128, :]))
                                TE.wait_ge(xa[i % 2].h, ld)
                                if i == 0:
                                    if s == 0:
                                        TE.wait_ge(sv.h, prep_end["sv"])
                                    else:
                                        TE.wait_ge(sv.h, seq_end["sv"])
                                        TE.wait_ge(ss.h, seq_end["ss"])
                                if i >= 2:
                                    TE.wait_ge(sv.h, xfdone[i - 2])
                                for b in range(8):
                                    ti = TE.transpose(
                                        out=tpa[2 * (i % 2) + b // 4][:, 128 * (b % 4):128 * (b % 4) + 128],
                                        in_=xstage[i % 2][:, 128 * b:128 * b + 128],
                                        identity=idf[:])
                                tpadone[i] = st.inc(ti)
                                VE.wait_ge(st.h, tpadone[i])
                                vop(VE.tensor_copy(
                                    out=xfeat_r[:, 0:4, 128 * u:128 * u + 128],
                                    in_=tpa[2 * (i % 2)][:].rearrange("p (c w) -> p c w", w=128)))
                                ev = VE.tensor_copy(
                                    out=xfeat_r[:, 4:8, 128 * u:128 * u + 128],
                                    in_=tpa[2 * (i % 2) + 1][:].rearrange("p (c w) -> p c w", w=128))
                                xfdone[i] = vop(ev)

                            for which in range(2):
                                wsl = slots[which]
                                dst = Q if which == 0 else K
                                for dch in range(NCH):
                                    TE.wait_ge(sv.h, xfdone[4 * sp + 3])
                                    TE.wait_ge(wsl_s[which].h, slotld[which])
                                    if dch >= 2:
                                        TE.wait_ge(sv.h, evQK[(sp, which, dch - 2)])
                                    elif which == 1:
                                        TE.wait_ge(sv.h, evQK[(sp, 0, 6 + dch)])
                                    elif sp > 0:
                                        TE.wait_ge(sv.h, evQK[(sp - 1, 1, 6 + dch)])
                                    for cch in range(NCH):
                                        mi = TE.matmul(
                                            out=qk_ps[dch % 2][:],
                                            lhsT=wsl[:, D * cch + 128 * dch:D * cch + 128 * dch + 128],
                                            rhs=xfeat[:, SPW * cch:SPW * cch + SPW],
                                            start=(cch == 0), stop=(cch == NCH - 1))
                                    mm = st.inc(mi)
                                    VE.wait_ge(st.h, mm)
                                    ev = VE.tensor_copy(
                                        out=dst[:, T * dch + SPW * sp:T * dch + SPW * sp + SPW],
                                        in_=qk_ps[dch % 2][:])
                                    evQK[(sp, which, dch)] = vop(ev)

                            for u in range(4):
                                i = 4 * sp + u
                                TE.wait_ge(wsl_s[2].h, slotld[2])
                                if i > 0:
                                    TE.wait_ge(sv.h, evV[i - 1])
                                for half in range(2):
                                    for cch in range(NCH):
                                        mi = TE.matmul(
                                            out=v_ps[:, 512 * half:512 * half + 512],
                                            lhsT=xfeat[:, SPW * cch + 128 * u:SPW * cch + 128 * u + 128],
                                            rhs=slots[2][:, D * cch + 512 * half:D * cch + 512 * half + 512],
                                            start=(cch == 0), stop=(cch == NCH - 1))
                                mm = st.inc(mi)
                                VE.wait_ge(st.h, mm)
                                ev = VE.tensor_copy(
                                    out=Vp[:].rearrange("p (i g w) -> p i g w", g=16, w=65)[:, i, :, 0:64],
                                    in_=v_ps[:].rearrange("p (g w) -> p g w", w=64))
                                evV[i] = vop(ev)
                        vones = vones0
                        pa_end = dict(st=st.v, sv=sv.v, ss=ss.v)

                    with ExitStack() as at:
                        dens = at.enter_context(nc.sbuf_tensor(f"dn{s}", [64, 512], F32))
                        recf = at.enter_context(nc.sbuf_tensor(f"rc{s}", [64, 512], F32))
                        hib = at.enter_context(nc.sbuf_tensor(f"hb{s}", [128, 512], BF16))
                        lob = at.enter_context(nc.sbuf_tensor(f"lb{s}", [128, 512], BF16))
                        selb = at.enter_context(nc.sbuf_tensor(f"se{s}", [128, 64 * 64], BF16))
                        tri = at.enter_context(nc.sbuf_tensor(f"tr{s}", [128, 128], BF16))
                        ebias = at.enter_context(nc.sbuf_tensor(f"eb{s}", [128, 1], F32))

                        SY.wait_ge(st.h, pa_end["st"])
                        SY.wait_ge(sv.h, pa_end["sv"])
                        cst.dma(SY.dma_start(out=selb[:], in_=sel_h))
                        atc = cst.dma(SY.dma_start(out=tri[:], in_=tri_h))
                        VE.wait_ge(cst.h, atc)
                        ebready = vop(VE.memset(ebias[:], EXP_BIAS))
                        vop(VE.memset(hib[64:128, :], 0.0))
                        vop(VE.memset(lob[64:128, :], 0.0))

                        qkdone = {}
                        pvdone = {}
                        expdone = {}
                        mkdone = {}
                        oddone = {}
                        odma = {}
                        ddma = {}
                        dsi = [0]
                        osi = [0]

                        with ExitStack() as at1:
                            P = [at1.enter_context(nc.sbuf_tensor(f"P{s}{i}", [128, T], BF16))
                                 for i in range(2)]
                            qh = [at1.enter_context(
                                nc.sbuf_tensor(f"qh{s}{i}", [128, T], BF16))
                                for i in range(2)]
                            vop(VE.memset(qh[0][64:128, :], 0.0))
                            vop(VE.memset(qh[1][0:64, :], 0.0))
                            qhdone = {}

                            def stage_qh(h):
                                poff_ = 64 * (h % 2)
                                qt_ = h // 2
                                qhdone[h] = vop(VE.tensor_copy(
                                    out=qh[h % 2][poff_:poff_ + 64, :],
                                    in_=Q[poff_:poff_ + 64, T * qt_:T * qt_ + T]))

                            stage_qh(0)
                            stage_qh(1)
                            dstage = [at1.enter_context(
                                nc.sbuf_tensor(f"dg{s}{i}", [128, 512], F32)) for i in range(2)]
                            ostage = [at1.enter_context(
                                nc.sbuf_tensor(f"og{s}{i}", [128, 512], BF16)) for i in range(2)]
                            S_half = [at1.enter_context(
                                nc.psum_tensor(f"S{s}{i}", [128, 1024], F32))
                                for i in range(2)]
                            PV_ps = at1.enter_context(
                                nc.psum_tensor(f"PV{s}", [65, T], F32))

                            def emit_pv(h, jj):
                                mi = None
                                for c in range(jj // 4, 4):
                                    a = max(512 * c, 128 * jj)
                                    b = 512 * c + 512
                                    mi = TE.matmul(
                                        out=PV_ps[0:65, a:b],
                                        lhsT=Vp[:, 1040 * jj + 65 * h:1040 * jj + 65 * h + 65],
                                        rhs=P[jj % 2][0:128, a - 128 * jj:b - 128 * jj],
                                        start=(jj == 0), stop=(jj == 4 * c + 3))
                                pvdone[(h, jj)] = st.inc(mi)

                            def emit_chunk_epilogue(h, c, qtile, poff):
                                VE.wait_ge(st.h, pvdone[(h, 4 * c + 3)])
                                cols = slice(T * qtile + 512 * c,
                                             T * qtile + 512 * c + 512)
                                if poff == 0:
                                    vop(VE.tensor_copy(
                                        out=O[0:64, cols],
                                        in_=PV_ps[0:64, 512 * c:512 * c + 512]))
                                else:
                                    ob = osi[0] % 2
                                    osi[0] += 1
                                    if osi[0] > 2:
                                        VE.wait_ge(ost_s[ob].h, odma[osi[0] - 3])
                                    vop(VE.tensor_copy(
                                        out=ostage[ob][0:64, :],
                                        in_=PV_ps[0:64, 512 * c:512 * c + 512]))
                                    myv = vop(VE.tensor_copy(
                                        out=ostage[ob][0:64, :],
                                        in_=PV_ps[0:64, 512 * c:512 * c + 512]))
                                    SY.wait_ge(sv.h, myv)
                                    odma[osi[0] - 1] = ost_s[ob].dma(SY.dma_start(
                                        out=O[64:128, cols], in_=ostage[ob][0:64, :]))
                                db = dsi[0] % 2
                                dsi[0] += 1
                                if dsi[0] > 2:
                                    VE.wait_ge(dst_s[db].h, ddma[dsi[0] - 3])
                                vop(VE.tensor_copy(
                                    out=dstage[db][64:65, :],
                                    in_=PV_ps[64:65, 512 * c:512 * c + 512]))
                                myv = vop(VE.tensor_copy(
                                    out=dstage[db][64:65, :],
                                    in_=PV_ps[64:65, 512 * c:512 * c + 512]))
                                SY.wait_ge(sv.h, myv)
                                p = 16 * c + h
                                ddma[dsi[0] - 1] = dst_s[db].dma(SY.dma_start(
                                    out=dens[p:p + 1, :], in_=dstage[db][64:65, :]))
                                oddone[h] = sv.v

                            for h in range(H):
                                qtile = h // 2
                                poff = 64 * (h % 2)
                                if h >= 1 and h + 1 < H:
                                    VE.wait_ge(st.h, qkdone[(h - 1, 15, 1)])
                                    stage_qh(h + 1)
                                for j in range(16):
                                    cs = list(range(j // 4, 4))
                                    nh = (len(cs) + 1) // 2
                                    lo, hi = cs[:nh], cs[nh:]
                                    off0 = 128 * (j % 4)
                                    for half, chs in ((0, lo), (1, hi)):
                                        if not chs:
                                            qkdone[(h, j, 1)] = qkdone[(h, j, 0)]
                                            continue
                                        key = (h, j - 1, half)
                                        if key in expdone:
                                            TE.wait_ge(ss.h, expdone[key])
                                        elif h > 0:
                                            TE.wait_ge(ss.h, expdone.get(
                                                (h - 1, 15, half),
                                                expdone[(h - 1, 15, 0)]))
                                        elif j == 0 and half == 0:
                                            TE.wait_ge(sv.h, vones)
                                        if j == 0 and half == 0:
                                            TE.wait_ge(sv.h, qhdone[h])
                                        mi = None
                                        for k_, c in enumerate(chs):
                                            a = max(512 * c, 128 * j)
                                            b = 512 * c + 512
                                            o = 512 * k_ + (off0 if c == cs[0] else 0)
                                            mi = TE.matmul(
                                                out=S_half[half][:, o:o + b - a],
                                                lhsT=K[:, T * qtile + 128 * j:
                                                       T * qtile + 128 * j + 128],
                                                rhs=qh[h % 2][:, a:b],
                                                start=True, stop=True)
                                        qkdone[(h, j, half)] = st.inc(mi)
                                    if j > 0:
                                        jj = j - 1
                                        TE.wait_ge(sv.h, mkdone[(h, jj)])
                                        TE.wait_ge(ss.h, expdone.get(
                                            (h, jj, 1), expdone[(h, jj, 0)]))
                                        if jj == 0 and h > 0:
                                            TE.wait_ge(sv.h, oddone[h - 1])
                                        emit_pv(h, jj)
                                        if jj % 4 == 3:
                                            emit_chunk_epilogue(h, jj // 4, qtile, poff)
                                    szA = 512 * len(lo) - off0
                                    SE.wait_ge(st.h, qkdone[(h, j, 0)])
                                    if h == 0 and j == 0:
                                        SE.wait_ge(sv.h, ebready)
                                    if (h, j - 2, 1) in pvdone or (h, j - 2) in pvdone:
                                        SE.wait_ge(st.h, pvdone[(h, j - 2)])
                                    elif h > 0 and j < 2:
                                        SE.wait_ge(st.h, pvdone[(h - 1, 14 + j)])
                                    ai = SE.activation(out=P[j % 2][:, 0:szA],
                                                       in_=S_half[0][:, off0:off0 + szA],
                                                       func=AF.Exp, bias=ebias[:],
                                                       scale=0.125)
                                    expdone[(h, j, 0)] = ss.inc(ai)
                                    if hi:
                                        szB = 512 * len(hi)
                                        SE.wait_ge(st.h, qkdone[(h, j, 1)])
                                        ai = SE.activation(
                                            out=P[j % 2][:, szA:szA + szB],
                                            in_=S_half[1][:, 0:szB],
                                            func=AF.Exp, bias=ebias[:], scale=0.125)
                                        expdone[(h, j, 1)] = ss.inc(ai)
                                    VE.wait_ge(ss.h, expdone[(h, j, 0)])
                                    m1 = sv.inc(VE.tensor_tensor(
                                        out=P[j % 2][:, 0:128],
                                        in0=P[j % 2][:, 0:128],
                                        in1=tri[:], op=ALU.mult))
                                    VE.wait_ge(sv.h, m1)
                                    mk = VE.tensor_tensor(out=P[j % 2][:, 0:128],
                                                          in0=P[j % 2][:, 0:128],
                                                          in1=tri[:], op=ALU.mult)
                                    mkdone[(h, j)] = vop(mk)
                                TE.wait_ge(sv.h, mkdone[(h, 15)])
                                emit_pv(h, 15)
                                emit_chunk_epilogue(h, 3, qtile, poff)

                        pre_div = dict(st=st.v, sv=sv.v, ss=ss.v)

                        VE.wait_ge(dst_s[0].h, dst_s[0].v)
                        VE.wait_ge(dst_s[1].h, dst_s[1].v)
                        VE.wait_ge(ost_s[0].h, ost_s[0].v)
                        VE.wait_ge(ost_s[1].h, ost_s[1].v)
                        vop(VE.reciprocal(out=recf[:], in_=dens[:]))
                        vop(VE.tensor_copy(out=hib[0:64, :], in_=recf[:]))
                        vop(VE.tensor_tensor(out=recf[:], in0=recf[:],
                                             in1=hib[0:64, :], op=ALU.subtract))
                        lo_ready = vop(VE.tensor_copy(out=lob[0:64, :], in_=recf[:]))

                        divdone = [0] * 4
                        with ExitStack() as bc:
                            bc_ps = [bc.enter_context(
                                nc.psum_tensor(f"bc{s}{i}", [128, 512], F32))
                                for i in range(2)]
                            dvd = {}
                            idx = 0
                            for c in range(4):
                                for h in range(H):
                                    p = 16 * c + h
                                    poff = 64 * (h % 2)
                                    qtile = h // 2
                                    TE.wait_ge(sv.h,
                                               lo_ready if idx < 2 else dvd[idx - 2])
                                    TE.matmul(out=bc_ps[idx % 2][poff:poff + 64, :],
                                              lhsT=selb[:, 64 * p:64 * p + 64],
                                              rhs=hib[:, :],
                                              start=True, stop=False)
                                    mi = TE.matmul(out=bc_ps[idx % 2][poff:poff + 64, :],
                                                   lhsT=selb[:, 64 * p:64 * p + 64],
                                                   rhs=lob[:, :],
                                                   start=False, stop=True)
                                    bcd = st.inc(mi)
                                    VE.wait_ge(st.h, bcd)
                                    cols = slice(T * qtile + 512 * c,
                                                 T * qtile + 512 * c + 512)
                                    ti = VE.tensor_tensor(
                                        out=O[poff:poff + 64, cols],
                                        in0=O[poff:poff + 64, cols],
                                        in1=bc_ps[idx % 2][poff:poff + 64, :],
                                        op=ALU.mult)
                                    dvd[idx] = vop(ti)
                                    idx += 1
                                divdone[c] = sv.v
                        at_end = dict(st=st.v, sv=sv.v, ss=ss.v)

                with ExitStack() as ff:
                    fslots = [ff.enter_context(
                        nc.sbuf_tensor(f"fs{s}{i}", [128, 8 * D], BF16))
                        for i in range(4)]
                    xstage2 = [ff.enter_context(
                        nc.sbuf_tensor(f"x2{s}{i}", [128, D], F32)) for i in range(2)]
                    x1span = [ff.enter_context(
                        nc.sbuf_tensor(f"x1{s}{i}", [128, D], F32)) for i in range(4)]
                    ybuf = [ff.enter_context(
                        nc.sbuf_tensor(f"yb{s}{i}", [128, D], F32)) for i in range(2)]
                    x1f = ff.enter_context(
                        nc.sbuf_tensor(f"xo{s}", [128, NCH * 513], F32))
                    x1f_r = x1f[:].rearrange("p (c w) -> p c w", w=513)
                    xr = ff.enter_context(
                        nc.sbuf_tensor(f"xr{s}", [128, NCH * SPW], BF16))
                    xk = ff.enter_context(
                        nc.sbuf_tensor(f"xk{s}", [128, NCH * SPW], BF16))
                    tmp = ff.enter_context(nc.sbuf_tensor(f"tm{s}", [128, SPW], F32))
                    kact = ff.enter_context(
                        nc.sbuf_tensor(f"ka{s}", [128, NCH * SPW], BF16))
                    krelu = [ff.enter_context(
                        nc.sbuf_tensor(f"kr{s}{i}", [128, SPW], F32)) for i in range(2)]
                    rbuf = [ff.enter_context(
                        nc.sbuf_tensor(f"rb{s}{i}", [128, D], F32)) for i in range(2)]
                    scr2 = [ff.enter_context(
                        nc.sbuf_tensor(f"sc{s}{i}", [128, D], F32)) for i in range(2)]
                    lnS = ff.enter_context(nc.sbuf_tensor(f"lS{s}", [128, 4], F32))
                    lnQ = ff.enter_context(nc.sbuf_tensor(f"lQ{s}", [128, 4], F32))
                    lnM = ff.enter_context(nc.sbuf_tensor(f"lM{s}", [128, 4], F32))
                    lnW = ff.enter_context(nc.sbuf_tensor(f"lW{s}", [128, 4], F32))
                    lnV = ff.enter_context(nc.sbuf_tensor(f"lV{s}", [128, 4], F32))
                    rstd = ff.enter_context(nc.sbuf_tensor(f"lR{s}", [128, 4], F32))
                    lnNB = ff.enter_context(nc.sbuf_tensor(f"lB{s}", [128, 4], F32))
                    murs = ff.enter_context(nc.sbuf_tensor(f"mr{s}", [128, 8], F32))
                    muks = ff.enter_context(nc.sbuf_tensor(f"mu{s}", [128, 8], F32))
                    dum_ps = ff.enter_context(
                        nc.psum_tensor(f"du{s}", [128, 512], F32))

                    def te_bridge(n):
                        for _ in range(n):
                            TE.matmul(out=dum_ps[:], lhsT=fslots[0][:, 0:128],
                                      rhs=fslots[0][:, 0:512],
                                      start=True, stop=True)

                    GP.wait_ge(st.h, pre_div["st"])
                    GP.wait_ge(sv.h, pre_div["sv"])
                    GP.wait_ge(ss.h, pre_div["ss"])
                    fsl = [wsl_s[i].dma(GP.dma_start(out=fslots[i][:],
                                                     in_=wscr[3 + i]))
                           for i in range(4)]
                    SY.wait_ge(st.h, at_end["st"])
                    SY.wait_ge(sv.h, at_end["sv"])
                    cst.dma(SY.dma_start(out=murs[:], in_=mur_h))
                    muld = cst.dma(SY.dma_start(out=muks[:], in_=muk_h))
                    VE.wait_ge(cst.h, muld)


                    def vsync(instr):
                        return vop(instr)
                        v = sv.inc(instr)
                        VE.wait_ge(sv.h, v)

                    def ln_batch():
                        vsync(VE.tensor_scalar(out=lnM[:], in0=lnS[:], scalar1=INVD,
                                               scalar2=None, op0=ALU.mult))
                        vsync(VE.tensor_tensor(out=lnW[:], in0=lnM[:], in1=lnS[:],
                                               op=ALU.mult))
                        vsync(VE.tensor_tensor(out=lnV[:], in0=lnQ[:], in1=lnW[:],
                                               op=ALU.subtract))
                        vop(VE.tensor_scalar(out=lnV[:], in0=lnV[:],
                                             scalar1=INVD, scalar2=None,
                                             op0=ALU.mult))
                        vop(VE.tensor_scalar(out=lnV[:], in0=lnV[:], scalar1=EPS,
                                             scalar2=None, op0=ALU.add))
                        vop(VE.tensor_scalar(out=rstd[:], in0=lnV[:], scalar1=0.5,
                                             scalar2=0.5, op0=ALU.mult, op1=ALU.add))
                        for _ in range(5):
                            vop(VE.reciprocal(out=lnW[:], in_=rstd[:]))
                            vop(VE.tensor_tensor(out=lnW[:], in0=lnV[:], in1=lnW[:],
                                                 op=ALU.mult))
                            vop(VE.tensor_tensor(out=lnW[:], in0=rstd[:], in1=lnW[:],
                                                 op=ALU.add))
                            vop(VE.tensor_scalar(out=rstd[:], in0=lnW[:], scalar1=0.5,
                                                 scalar2=None, op0=ALU.mult))
                        vop(VE.reciprocal(out=rstd[:], in_=rstd[:]))
                        vop(VE.scalar_tensor_tensor(out=lnNB[:], in0=lnM[:],
                                                    scalar=-1.0, in1=rstd[:],
                                                    op0=ALU.mult, op1=ALU.mult))
                        return sv.v

                    z1done = {}
                    x1fs = {}
                    sq1st = {}
                    sq2st = {}
                    x1done = {}
                    tpfdone = {}
                    x1fdone = {}
                    mixdone = {}
                    ktdone = {}
                    reludone = {}
                    sqdone = {}
                    sgdone = {}
                    gdone = {}
                    ydma = {}
                    xrld = {}
                    a2done = {}
                    yv2a = {}

                    for sp in range(NSP):
                        with ExitStack() as ps1:
                            o_ps = [ps1.enter_context(
                                nc.psum_tensor(f"op{s}{sp}{i}", [128, D], F32))
                                for i in range(2)]
                            tpf = [ps1.enter_context(
                                nc.psum_tensor(f"tf{s}{sp}{i}", [128, 512], F32))
                                for i in range(2)]
                            for u in range(4):
                                i = 4 * sp + u
                                t0 = 128 * i
                                if i >= 2:
                                    GP.wait_ge(sv.h, z1done[i - 2])
                                xrld[i] = xg[i % 2].dma(GP.dma_start(
                                    out=xstage2[i % 2][:],
                                    in_=x_h[s, t0:t0 + 128, :]))
                                TE.wait_ge(sv.h, divdone[3 if sp == 0 else sp])
                                TE.wait_ge(wsl_s[0].h, fsl[0])
                                if u == 0 and sp > 0:
                                    TE.wait_ge(sv.h, gdone[4 * sp - 1])
                                    TE.wait_ge(ss.h, sgdone[4 * sp - 1])
                                if u >= 2:
                                    TE.wait_ge(sv.h, z1done[i - 2])
                                mi = None
                                for ch in range(NCH):
                                    for half in range(2):
                                        mi = TE.matmul(
                                            out=o_ps[u % 2][:, 512 * half:512 * half + 512],
                                            lhsT=O[:, T * ch + t0:T * ch + t0 + 128],
                                            rhs=fslots[0][:, D * ch + 512 * half:D * ch + 512 * half + 512],
                                            start=(ch == 0), stop=(ch == NCH - 1))
                                omm = st.inc(mi)
                                VE.wait_ge(st.h, omm)
                                VE.wait_ge(xg[i % 2].h, xrld[i])
                                if sp > 0:
                                    VE.wait_ge(ss.h, a2done[4 * (sp - 1) + u])
                                ti = VE.tensor_tensor(out=x1span[u][:],
                                                      in0=o_ps[u % 2][:],
                                                      in1=xstage2[i % 2][:], op=ALU.add)
                                z1done[i] = vop(ti)
                                SE.wait_ge(sv.h, z1done[i])
                                if u >= 2:
                                    SE.wait_ge(ss.h, sq1st[i - 2])
                                if sp > 0:
                                    SE.wait_ge(ss.h,
                                               yv2a[4 * (sp - 1) + 2 + (u % 2)])
                                q1 = ss.inc(SE.activation(
                                    out=scr2[u % 2][:],
                                    in_=x1span[u][:], func=AF.Square,
                                    accum_out=lnQ[:, u:u + 1]))
                                SE.wait_ge(ss.h, q1)
                                sq = SE.activation(out=scr2[u % 2][:],
                                                   in_=x1span[u][:],
                                                   func=AF.Identity,
                                                   accum_out=lnS[:, u:u + 1])
                                sq1st[i] = ss.inc(sq)
                            te_bridge(64)
                            VE.wait_ge(ss.h, sq1st[4 * sp + 3])
                            ln1r = ln_batch()
                            for u in range(4):
                                i = 4 * sp + u
                                SE.wait_ge(sv.h, ln1r)
                                ai = SE.activation(out=x1span[u][:],
                                                   in_=x1span[u][:],
                                                   func=AF.Identity,
                                                   bias=lnNB[:, u:u + 1],
                                                   scale=rstd[:, u:u + 1])
                                x1done[i] = ss.inc(ai)
                                if u == 0:
                                    if sp == 0:
                                        vsync(VE.memset(x1f_r[:, :, 0:1], 0.0))
                                    else:
                                        ec = VE.tensor_copy(out=x1f_r[:, :, 0:1],
                                                            in_=x1f_r[:, :, 512:513])
                                        vsync(ec)
                                TE.wait_ge(ss.h, x1done[i])
                                if u >= 1:
                                    TE.wait_ge(sv.h, x1fdone[i - 1])
                                    TE.wait_ge(ss.h, x1fs[i - 1])
                                for b in range(8):
                                    ti = TE.transpose(
                                        out=tpf[b // 4][:, 128 * (b % 4):128 * (b % 4) + 128],
                                        in_=x1span[u][:, 128 * b:128 * b + 128],
                                        identity=idf[:])
                                tpfdone[i] = st.inc(ti)
                                SE.wait_ge(st.h, tpfdone[i])
                                sc1 = SE.copy(
                                    out=x1f_r[:, 0:4, 1 + 128 * u:1 + 128 * u + 128],
                                    in_=tpf[0][:].rearrange("p (c w) -> p c w", w=128))
                                x1fs[i] = ss.inc(sc1)
                                VE.wait_ge(st.h, tpfdone[i])
                                ev = VE.tensor_copy(
                                    out=x1f_r[:, 4:8, 1 + 128 * u:1 + 128 * u + 128],
                                    in_=tpf[1][:].rearrange("p (c w) -> p c w", w=128))
                                x1fdone[i] = vop(ev)
                            VE.wait_ge(ss.h, x1fs[4 * sp + 3])
                            for ch in range(NCH):
                                if ch >= 1:
                                    VE.wait_ge(sv.h, mixdone[(sp, ch - 1)])
                                vsync(VE.tensor_tensor(out=tmp[:],
                                                       in0=x1f_r[:, ch, 1:513],
                                                       in1=x1f_r[:, ch, 0:512],
                                                       op=ALU.subtract))
                                sv.inc(VE.scalar_tensor_tensor(
                                    out=xr[:, SPW * ch:SPW * ch + SPW], in0=tmp[:],
                                    scalar=murs[:, ch:ch + 1],
                                    in1=x1f_r[:, ch, 0:512],
                                    op0=ALU.mult, op1=ALU.add))
                                mk = VE.scalar_tensor_tensor(
                                    out=xk[:, SPW * ch:SPW * ch + SPW], in0=tmp[:],
                                    scalar=muks[:, ch:ch + 1],
                                    in1=x1f_r[:, ch, 0:512],
                                    op0=ALU.mult, op1=ALU.add)
                                mixdone[(sp, ch)] = sv.inc(mk)
                            te_bridge(40)
                        with ExitStack() as ps2:
                            k_ps = [ps2.enter_context(
                                nc.psum_tensor(f"kp{s}{sp}{i}", [128, 512], F32))
                                for i in range(2)]
                            r_ps = ps2.enter_context(
                                nc.psum_tensor(f"rp{s}{sp}", [128, D], F32))
                            v2_ps = ps2.enter_context(
                                nc.psum_tensor(f"v2{s}{sp}", [128, D], F32))
                            for kch in range(NCH):
                                TE.wait_ge(sv.h, mixdone[(sp, NCH - 1)])
                                TE.wait_ge(wsl_s[2].h, fsl[2])
                                if kch >= 2:
                                    TE.wait_ge(sv.h, sqdone[(sp, kch - 2)])
                                mi = None
                                for cch in range(NCH):
                                    mi = TE.matmul(
                                        out=k_ps[kch % 2][:],
                                        lhsT=fslots[2][:, D * cch + 128 * kch:D * cch + 128 * kch + 128],
                                        rhs=xk[:, SPW * cch:SPW * cch + SPW],
                                        start=(cch == 0), stop=(cch == NCH - 1))
                                ktdone[(sp, kch)] = st.inc(mi)
                                SE.wait_ge(st.h, ktdone[(sp, kch)])
                                if kch >= 2:
                                    SE.wait_ge(sv.h, sqdone[(sp, kch - 2)])
                                ri = SE.activation(out=krelu[kch % 2][:],
                                                   in_=k_ps[kch % 2][:], func=AF.Relu)
                                reludone[(sp, kch)] = ss.inc(ri)
                                VE.wait_ge(ss.h, reludone[(sp, kch)])
                                sq = VE.scalar_tensor_tensor(
                                    out=kact[:, SPW * kch:SPW * kch + SPW],
                                    in0=krelu[kch % 2][:], scalar=0.5,
                                    in1=k_ps[kch % 2][:],
                                    op0=ALU.mult, op1=ALU.mult)
                                sqdone[(sp, kch)] = sv.inc(sq)
                            for u in range(4):
                                i = 4 * sp + u
                                t0 = 128 * i
                                TE.wait_ge(wsl_s[1].h, fsl[1])
                                if u >= 1:
                                    TE.wait_ge(ss.h, sgdone[i - 1])
                                mi = None
                                for cch in range(NCH):
                                    for half in range(2):
                                        mi = TE.matmul(
                                            out=r_ps[:, 512 * half:512 * half + 512],
                                            lhsT=xr[:, SPW * cch + 128 * u:SPW * cch + 128 * u + 128],
                                            rhs=fslots[1][:, D * cch + 512 * half:D * cch + 512 * half + 512],
                                            start=(cch == 0), stop=(cch == NCH - 1))
                                rmm = st.inc(mi)
                                SE.wait_ge(st.h, rmm)
                                if u >= 2:
                                    SE.wait_ge(sv.h, gdone[i - 2])
                                si = SE.activation(out=rbuf[u % 2][:], in_=r_ps[:],
                                                   func=AF.Tanh, scale=0.5)
                                sgdone[i] = ss.inc(si)
                                TE.wait_ge(sv.h, sqdone[(sp, NCH - 1)])
                                TE.wait_ge(wsl_s[3].h, fsl[3])
                                if u >= 1:
                                    TE.wait_ge(sv.h, gdone[i - 1])
                                mi = None
                                for cch in range(NCH):
                                    for half in range(2):
                                        mi = TE.matmul(
                                            out=v2_ps[:, 512 * half:512 * half + 512],
                                            lhsT=kact[:, SPW * cch + 128 * u:SPW * cch + 128 * u + 128],
                                            rhs=fslots[3][:, D * cch + 512 * half:D * cch + 512 * half + 512],
                                            start=(cch == 0), stop=(cch == NCH - 1))
                                v2mm = st.inc(mi)
                                VE.wait_ge(st.h, v2mm)
                                VE.wait_ge(ss.h, sgdone[i])
                                VE.wait_ge(ss.h, sq1st[i])
                                vsync(VE.tensor_tensor(out=scr2[u % 2][:],
                                                       in0=rbuf[u % 2][:],
                                                       in1=v2_ps[:], op=ALU.mult))
                                vsync(VE.tensor_tensor(out=scr2[u % 2][:],
                                                       in0=scr2[u % 2][:],
                                                       in1=v2_ps[:], op=ALU.add))
                                gi = VE.tensor_tensor(out=x1span[u][:],
                                                      in0=scr2[u % 2][:],
                                                      in1=x1span[u][:], op=ALU.add)
                                gdone[i] = vop(gi)
                                SE.wait_ge(sv.h, gdone[i])
                                if u >= 2:
                                    SE.wait_ge(ss.h, sq2st[i - 2])
                                q1 = ss.inc(SE.activation(
                                    out=scr2[u % 2][:],
                                    in_=x1span[u][:], func=AF.Square,
                                    accum_out=lnQ[:, u:u + 1]))
                                SE.wait_ge(ss.h, q1)
                                sq = SE.activation(out=scr2[u % 2][:],
                                                   in_=x1span[u][:],
                                                   func=AF.Identity,
                                                   accum_out=lnS[:, u:u + 1])
                                sq2st[i] = ss.inc(sq)
                            VE.wait_ge(ss.h, sq2st[4 * sp + 3])
                            ln2r = ln_batch()
                            for u in range(4):
                                i = 4 * sp + u
                                t0 = 128 * i
                                SE.wait_ge(sv.h, ln2r)
                                if i >= 2:
                                    SE.wait_ge(ysem[(u + 2) % 4].h, ydma[i - 2])
                                elif s > 0:
                                    SE.wait_ge(ysem[u + 2].h,
                                               ydma_prev[14 + u])
                                a2 = ss.inc(SE.activation(
                                    out=ybuf[u % 2][:], in_=x1span[u][:],
                                    func=AF.Identity,
                                    bias=lnNB[:, u:u + 1],
                                    scale=rstd[:, u:u + 1]))
                                a2done[i] = a2
                                SE.wait_ge(ss.h, a2)
                                sp2 = SE.copy(out=scr2[u % 2][:],
                                              in_=ybuf[u % 2][:])
                                yv2 = ss.inc(sp2)
                                yv2a[i] = yv2
                                SY.wait_ge(ss.h, yv2)
                                ydma[i] = ysem[u].dma(SY.dma_start(
                                    out=y_h[s, t0:t0 + 128, :], in_=ybuf[u % 2][:]))
                    te_bridge(144)
                    ydma_prev = dict(ydma)
                    seq_end = dict(st=st.v, sv=sv.v, ss=ss.v)

        for yy in ysem:
            SY.wait_ge(yy.h, yy.v)
            TE.wait_ge(yy.h, yy.v)
        for c_ in (cst, xa[0], xa[1], dst_s[0], dst_s[1], ost_s[0], ost_s[1]):
            SY.wait_ge(c_.h, c_.v)
        for c_ in (wst_s[0], wst_s[1], wsl_s[0], wsl_s[1], wsl_s[2], wsl_s[3],
                   wscr_s, xg[0], xg[1]):
            GP.wait_ge(c_.h, c_.v)
    return nc


_NC_CACHE = {}


def _get_nc():
    if "nc" not in _NC_CACHE:
        _NC_CACHE["nc"] = build_nc()
    return _NC_CACHE["nc"]


def kernel(L0_attn_in_features, L0_attn_q_proj_weight, L0_attn_k_proj_weight,
           L0_attn_v_proj_weight, L0_attn_o_proj_weight, L0_ffn_W_r,
           L0_ffn_W_k, L0_ffn_W_v, L0_ffn_mu_r, L0_ffn_mu_k):
    x = np.ascontiguousarray(np.asarray(L0_attn_in_features, dtype=np.float32))
    ws = {
        "wq": np.ascontiguousarray(np.asarray(L0_attn_q_proj_weight, np.float32)),
        "wk": np.ascontiguousarray(np.asarray(L0_attn_k_proj_weight, np.float32)),
        "wv": np.ascontiguousarray(np.asarray(L0_attn_v_proj_weight, np.float32)),
        "wo": np.ascontiguousarray(np.asarray(L0_attn_o_proj_weight, np.float32)),
        "wr": np.ascontiguousarray(np.asarray(L0_ffn_W_r, np.float32)),
        "wkf": np.ascontiguousarray(np.asarray(L0_ffn_W_k, np.float32)),
        "wvf": np.ascontiguousarray(np.asarray(L0_ffn_W_v, np.float32)),
    }
    mur = np.ascontiguousarray(np.asarray(L0_ffn_mu_r, np.float32).reshape(8, 128).T)
    muk = np.ascontiguousarray(np.asarray(L0_ffn_mu_k, np.float32).reshape(8, 128).T)
    consts = {
        "mur": mur, "muk": muk,
        "idf": np.eye(128, dtype=np.float32),
        "tri": np.triu(np.ones((128, 128), dtype=np.float32)).astype(ml_dtypes.bfloat16),
        "sel": _sel_const(),
    }
    nc = _get_nc()
    in_maps = []
    for i in range(8):
        m = {"x": x[2 * i:2 * i + 2]}
        m.update(ws)
        m.update(consts)
        in_maps.append(m)
    res = run_bass_kernel_spmd(nc, in_maps, core_ids=list(range(8)))
    return np.concatenate([np.asarray(res.results[i]["y"]) for i in range(8)],
                          axis=0)


# version 27
# speedup vs baseline: 1.0072x; 1.0072x over previous
import numpy as np
import ml_dtypes
from contextlib import ExitStack

import concourse.bass as bass
import concourse.mybir as mybir
from concourse.bass_utils import run_bass_kernel_spmd

F32 = mybir.dt.float32
BF16 = mybir.dt.bfloat16
I32 = mybir.dt.int32
AF = mybir.ActivationFunctionType
ALU = mybir.AluOpType
AXX = mybir.AxisListType.X

SEQS = 2
T = 2048
D = 1024
NT = 16
NCH = 8
NSP = 4
SPW = 512
H = 16
EXP_BIAS = -12.0
EPS = 1e-5
INVD = 1.0 / D
MAGIC_F = float(np.frombuffer(np.uint32(0x5F3759DF).tobytes(), np.float32)[0])

WNAMES = ["wq", "wk", "wv", "wo", "wr", "wkf", "wvf"]


def _sel_const():
    sel = np.zeros((128, 64 * 64), dtype=ml_dtypes.bfloat16)
    for p in range(64):
        sel[p, 64 * p:64 * p + 64] = 1
    return sel


class Ctr:

    def __init__(self, ctx, nc, name):
        self.h = ctx.enter_context(nc.semaphore(name))
        self.v = 0

    def inc(self, instr, by=1):
        instr.then_inc(self.h, by)
        self.v += by
        return self.v

    def dma(self, instr):
        return self.inc(instr, 16)


def build_nc():
    nc = bass.Bass("TRN2", target_bir_lowering=False)
    TE, VE, SE, SY, GP = nc.tensor, nc.vector, nc.scalar, nc.sync, nc.gpsimd

    x_h = nc.dram_tensor("x", [SEQS, T, D], F32, kind="ExternalInput").ap()
    w_h = {n: nc.dram_tensor(n, [D, D], F32, kind="ExternalInput").ap()
           for n in WNAMES}
    mur_h = nc.dram_tensor("mur", [128, 8], F32, kind="ExternalInput").ap()
    muk_h = nc.dram_tensor("muk", [128, 8], F32, kind="ExternalInput").ap()
    idf_h = nc.dram_tensor("idf", [128, 128], F32, kind="ExternalInput").ap()
    tri_h = nc.dram_tensor("tri", [128, 128], BF16, kind="ExternalInput").ap()
    sel_h = nc.dram_tensor("sel", [128, 64 * 64], BF16, kind="ExternalInput").ap()
    y_h = nc.dram_tensor("y", [SEQS, T, D], F32, kind="ExternalOutput").ap()
    wscr = nc.dram_tensor("wscr", [7, 128, 8 * D], BF16, kind="Internal").ap()

    CTX = ExitStack()
    with CTX:
        idf = CTX.enter_context(nc.sbuf_tensor("idf_s", [128, 128], F32))

        st = Ctr(CTX, nc, "st")
        ss = Ctr(CTX, nc, "ss")
        sv = Ctr(CTX, nc, "sv")
        cst = Ctr(CTX, nc, "cst")
        xa = [Ctr(CTX, nc, f"xa{i}") for i in range(2)]
        xg = [Ctr(CTX, nc, f"xg{i}") for i in range(2)]
        ysem = [Ctr(CTX, nc, f"y{i}") for i in range(4)]
        dst_s = [Ctr(CTX, nc, f"dsx{i}") for i in range(2)]
        ost_s = [Ctr(CTX, nc, f"osx{i}") for i in range(2)]
        wst_s = [Ctr(CTX, nc, f"wsx{i}") for i in range(2)]
        wsl_s = [Ctr(CTX, nc, f"wl{i}") for i in range(4)]
        wscr_s = Ctr(CTX, nc, "wsc")

        def vop(instr):
            v = sv.inc(instr)
            VE.wait_ge(sv.h, v)
            return v

        cst.dma(SY.dma_start(out=idf[:], in_=idf_h))
        consts_loaded = cst.v

        scrdone = {}
        with ExitStack() as prep:
            wstage = [prep.enter_context(nc.sbuf_tensor(f"wst{i}", [128, D], F32))
                      for i in range(2)]
            asm = prep.enter_context(nc.sbuf_tensor("wasm", [128, 8 * D], BF16))
            asm_r = asm[:].rearrange("p (c w) -> p c w", w=D)
            pp = [prep.enter_context(nc.psum_tensor(f"pp{i}", [128, 512], F32))
                  for i in range(4)]
            warm = prep.enter_context(nc.sbuf_tensor("warm", [128, 512], F32))
            trip = prep.enter_context(nc.sbuf_tensor("trip", [128, 128], BF16))
            dumP = prep.enter_context(nc.psum_tensor("dup", [128, 512], F32))
            tpdone = {}
            evdone = {}
            WT = 8
            trld = wsl_s[3].dma(SY.dma_start(out=trip[:], in_=tri_h))
            TE.wait_ge(wsl_s[3].h, trld)

            def prep_bridge(n):
                for _ in range(n):
                    TE.matmul(out=dumP[:, 0:128], lhsT=trip[:],
                              rhs=trip[:], start=True, stop=True)

            SE.wait_ge(cst.h, consts_loaded)
            for _ in range(8):
                wv = ss.inc(SE.activation(out=warm[:, 0:128], in_=idf[:],
                                          func=AF.Exp))
                SE.wait_ge(ss.h, wv)
            for wi, wn in enumerate(WNAMES):
                for t in range(WT):
                    if t >= 2:
                        GP.wait_ge(st.h, tpdone[(wi, t - 2)])
                    elif wi > 0:
                        GP.wait_ge(st.h, tpdone[(wi - 1, WT - 2 + t)])
                    ld = wst_s[t % 2].dma(GP.dma_start(
                        out=wstage[t % 2][:],
                        in_=w_h[wn][128 * t:128 * t + 128, :]))
                    TE.wait_ge(wst_s[t % 2].h, ld)
                    if wi == 0 and t == 0:
                        TE.wait_ge(cst.h, consts_loaded)
                    if t >= 2:
                        TE.wait_ge(sv.h, evdone[(wi, t - 2)])
                    elif wi > 0:
                        TE.wait_ge(sv.h, evdone[(wi - 1, WT - 2 + t)])
                    for b in range(8):
                        ti = TE.transpose(
                            out=pp[2 * (t % 2) + b // 4][:, 128 * (b % 4):128 * (b % 4) + 128],
                            in_=wstage[t % 2][:, 128 * b:128 * b + 128],
                            identity=idf[:])
                    tpdone[(wi, t)] = st.inc(ti)
                    prep_bridge(14)
                    VE.wait_ge(st.h, tpdone[(wi, t)])
                    if t == 0 and wi > 0:
                        VE.wait_ge(wscr_s.h, scrdone[wi - 1])
                    vop(VE.tensor_copy(out=asm_r[:, 0:4, 128 * t:128 * t + 128],
                                       in_=pp[2 * (t % 2)][:].rearrange("p (c w) -> p c w", w=128)))
                    ev = VE.tensor_copy(out=asm_r[:, 4:8, 128 * t:128 * t + 128],
                                        in_=pp[2 * (t % 2) + 1][:].rearrange("p (c w) -> p c w", w=128))
                    evdone[(wi, t)] = vop(ev)
                GP.wait_ge(sv.h, evdone[(wi, WT - 1)])
                scrdone[wi] = wscr_s.dma(GP.dma_start(out=wscr[wi], in_=asm[:]))
            prep_end = dict(st=st.v, sv=sv.v, ss=ss.v)

        seq_end = None
        ydma_prev = None
        for s in range(SEQS):
            with ExitStack() as seqctx:
                O = seqctx.enter_context(
                    nc.sbuf_tensor(f"O{s}", [128, NCH * T], BF16))
                with ExitStack() as qs:
                    Q = qs.enter_context(nc.sbuf_tensor(f"Q{s}", [128, NCH * T], BF16))
                    K = qs.enter_context(nc.sbuf_tensor(f"K{s}", [128, NCH * T], BF16))
                    Vp = qs.enter_context(nc.sbuf_tensor(f"V{s}", [128, NT * 1040], BF16))
                    if s == 0:
                        VE.wait_ge(st.h, prep_end["st"])
                        VE.wait_ge(ss.h, prep_end["ss"])
                        VE.wait_ge(wscr_s.h, scrdone[len(WNAMES) - 1])
                    else:
                        VE.wait_ge(st.h, seq_end["st"])
                        VE.wait_ge(ss.h, seq_end["ss"])
                        for yy in ysem:
                            VE.wait_ge(yy.h, yy.v)
                    vop(VE.memset(
                        Vp[:].rearrange("p (i g w) -> p i g w", g=16, w=65)[:, :, :, 64:65],
                        1.0))
                    vones0 = vop(VE.memset(
                        Vp[:].rearrange("p (i g w) -> p i g w", g=16, w=65)[:, :, :, 64:65],
                        1.0))

                    with ExitStack() as pa:
                        slots = [pa.enter_context(
                            nc.sbuf_tensor(f"sl{s}{i}", [128, 8 * D], BF16))
                            for i in range(3)]
                        xstage = [pa.enter_context(
                            nc.sbuf_tensor(f"xs{s}{i}", [128, D], F32)) for i in range(2)]
                        xfeat = pa.enter_context(
                            nc.sbuf_tensor(f"xf{s}", [128, NCH * SPW], BF16))
                        xfeat_r = xfeat[:].rearrange("p (c w) -> p c w", w=SPW)
                        tpa = [pa.enter_context(
                            nc.psum_tensor(f"tp{s}{i}", [128, 512], F32)) for i in range(4)]
                        qk_ps = [pa.enter_context(
                            nc.psum_tensor(f"qk{s}{i}", [128, 512], F32)) for i in range(2)]
                        v_ps = pa.enter_context(nc.psum_tensor(f"vv{s}", [128, 1024], F32))

                        if s == 0:
                            GP.wait_ge(st.h, prep_end["st"])
                            GP.wait_ge(sv.h, prep_end["sv"])
                            GP.wait_ge(wscr_s.h, scrdone[2])
                        else:
                            GP.wait_ge(st.h, seq_end["st"])
                            GP.wait_ge(sv.h, seq_end["sv"])
                            GP.wait_ge(ss.h, seq_end["ss"])
                            SY.wait_ge(st.h, seq_end["st"])
                            SY.wait_ge(sv.h, seq_end["sv"])
                            SY.wait_ge(ss.h, seq_end["ss"])
                        slotld = [wsl_s[i].dma(GP.dma_start(out=slots[i][:],
                                                             in_=wscr[i]))
                                  for i in range(3)]

                        tpadone = {}
                        xfdone = {}
                        evQK = {}
                        evV = {}
                        for sp in range(NSP):
                            for u in range(4):
                                i = 4 * sp + u
                                if i >= 2:
                                    SY.wait_ge(st.h, tpadone[i - 2])
                                ld = xa[i % 2].dma(SY.dma_start(
                                    out=xstage[i % 2][:],
                                    in_=x_h[s, 128 * i:128 * i + 128, :]))
                                TE.wait_ge(xa[i % 2].h, ld)
                                if i == 0:
                                    if s == 0:
                                        TE.wait_ge(sv.h, prep_end["sv"])
                                    else:
                                        TE.wait_ge(sv.h, seq_end["sv"])
                                        TE.wait_ge(ss.h, seq_end["ss"])
                                if i >= 2:
                                    TE.wait_ge(sv.h, xfdone[i - 2])
                                for b in range(8):
                                    ti = TE.transpose(
                                        out=tpa[2 * (i % 2) + b // 4][:, 128 * (b % 4):128 * (b % 4) + 128],
                                        in_=xstage[i % 2][:, 128 * b:128 * b + 128],
                                        identity=idf[:])
                                tpadone[i] = st.inc(ti)
                                VE.wait_ge(st.h, tpadone[i])
                                vop(VE.tensor_copy(
                                    out=xfeat_r[:, 0:4, 128 * u:128 * u + 128],
                                    in_=tpa[2 * (i % 2)][:].rearrange("p (c w) -> p c w", w=128)))
                                ev = VE.tensor_copy(
                                    out=xfeat_r[:, 4:8, 128 * u:128 * u + 128],
                                    in_=tpa[2 * (i % 2) + 1][:].rearrange("p (c w) -> p c w", w=128))
                                xfdone[i] = vop(ev)

                            for which in range(2):
                                wsl = slots[which]
                                dst = Q if which == 0 else K
                                for dch in range(NCH):
                                    TE.wait_ge(sv.h, xfdone[4 * sp + 3])
                                    TE.wait_ge(wsl_s[which].h, slotld[which])
                                    if dch >= 2:
                                        TE.wait_ge(sv.h, evQK[(sp, which, dch - 2)])
                                    elif which == 1:
                                        TE.wait_ge(sv.h, evQK[(sp, 0, 6 + dch)])
                                    elif sp > 0:
                                        TE.wait_ge(sv.h, evQK[(sp - 1, 1, 6 + dch)])
                                    for cch in range(NCH):
                                        mi = TE.matmul(
                                            out=qk_ps[dch % 2][:],
                                            lhsT=wsl[:, D * cch + 128 * dch:D * cch + 128 * dch + 128],
                                            rhs=xfeat[:, SPW * cch:SPW * cch + SPW],
                                            start=(cch == 0), stop=(cch == NCH - 1))
                                    mm = st.inc(mi)
                                    VE.wait_ge(st.h, mm)
                                    ev = VE.tensor_copy(
                                        out=dst[:, T * dch + SPW * sp:T * dch + SPW * sp + SPW],
                                        in_=qk_ps[dch % 2][:])
                                    evQK[(sp, which, dch)] = vop(ev)

                            for u in range(4):
                                i = 4 * sp + u
                                TE.wait_ge(wsl_s[2].h, slotld[2])
                                if i > 0:
                                    TE.wait_ge(sv.h, evV[i - 1])
                                for half in range(2):
                                    for cch in range(NCH):
                                        mi = TE.matmul(
                                            out=v_ps[:, 512 * half:512 * half + 512],
                                            lhsT=xfeat[:, SPW * cch + 128 * u:SPW * cch + 128 * u + 128],
                                            rhs=slots[2][:, D * cch + 512 * half:D * cch + 512 * half + 512],
                                            start=(cch == 0), stop=(cch == NCH - 1))
                                mm = st.inc(mi)
                                VE.wait_ge(st.h, mm)
                                ev = VE.tensor_copy(
                                    out=Vp[:].rearrange("p (i g w) -> p i g w", g=16, w=65)[:, i, :, 0:64],
                                    in_=v_ps[:].rearrange("p (g w) -> p g w", w=64))
                                evV[i] = vop(ev)
                        vones = vones0
                        pa_end = dict(st=st.v, sv=sv.v, ss=ss.v)

                    with ExitStack() as at:
                        dens = at.enter_context(nc.sbuf_tensor(f"dn{s}", [64, 512], F32))
                        recf = at.enter_context(nc.sbuf_tensor(f"rc{s}", [64, 512], F32))
                        hib = at.enter_context(nc.sbuf_tensor(f"hb{s}", [128, 512], BF16))
                        lob = at.enter_context(nc.sbuf_tensor(f"lb{s}", [128, 512], BF16))
                        selb = at.enter_context(nc.sbuf_tensor(f"se{s}", [128, 64 * 64], BF16))
                        tri = at.enter_context(nc.sbuf_tensor(f"tr{s}", [128, 128], BF16))
                        ebias = at.enter_context(nc.sbuf_tensor(f"eb{s}", [128, 1], F32))

                        SY.wait_ge(st.h, pa_end["st"])
                        SY.wait_ge(sv.h, pa_end["sv"])
                        cst.dma(SY.dma_start(out=selb[:], in_=sel_h))
                        atc = cst.dma(SY.dma_start(out=tri[:], in_=tri_h))
                        VE.wait_ge(cst.h, atc)
                        ebready = vop(VE.memset(ebias[:], EXP_BIAS))
                        vop(VE.memset(hib[64:128, :], 0.0))
                        vop(VE.memset(lob[64:128, :], 0.0))

                        qkdone = {}
                        pvdone = {}
                        expdone = {}
                        mkdone = {}
                        oddone = {}
                        odma = {}
                        ddma = {}
                        dsi = [0]
                        osi = [0]

                        with ExitStack() as at1:
                            P = [at1.enter_context(nc.sbuf_tensor(f"P{s}{i}", [128, T], BF16))
                                 for i in range(2)]
                            qh = [at1.enter_context(
                                nc.sbuf_tensor(f"qh{s}{i}", [128, T], BF16))
                                for i in range(2)]
                            vop(VE.memset(qh[0][64:128, :], 0.0))
                            vop(VE.memset(qh[1][0:64, :], 0.0))
                            qhdone = {}

                            def stage_qh(h):
                                poff_ = 64 * (h % 2)
                                qt_ = h // 2
                                qhdone[h] = vop(VE.tensor_copy(
                                    out=qh[h % 2][poff_:poff_ + 64, :],
                                    in_=Q[poff_:poff_ + 64, T * qt_:T * qt_ + T]))

                            stage_qh(0)
                            stage_qh(1)
                            dstage = [at1.enter_context(
                                nc.sbuf_tensor(f"dg{s}{i}", [128, 512], F32)) for i in range(2)]
                            ostage = [at1.enter_context(
                                nc.sbuf_tensor(f"og{s}{i}", [128, 512], BF16)) for i in range(2)]
                            S_half = [at1.enter_context(
                                nc.psum_tensor(f"S{s}{i}", [128, 1024], F32))
                                for i in range(2)]
                            PV_ps = at1.enter_context(
                                nc.psum_tensor(f"PV{s}", [65, T], F32))

                            def emit_pv(h, jj):
                                mi = None
                                for c in range(jj // 4, 4):
                                    a = max(512 * c, 128 * jj)
                                    b = 512 * c + 512
                                    mi = TE.matmul(
                                        out=PV_ps[0:65, a:b],
                                        lhsT=Vp[:, 1040 * jj + 65 * h:1040 * jj + 65 * h + 65],
                                        rhs=P[jj % 2][0:128, a - 128 * jj:b - 128 * jj],
                                        start=(jj == 0), stop=(jj == 4 * c + 3))
                                pvdone[(h, jj)] = st.inc(mi)

                            def emit_chunk_epilogue(h, c, qtile, poff):
                                VE.wait_ge(st.h, pvdone[(h, 4 * c + 3)])
                                cols = slice(T * qtile + 512 * c,
                                             T * qtile + 512 * c + 512)
                                if poff == 0:
                                    vop(VE.tensor_copy(
                                        out=O[0:64, cols],
                                        in_=PV_ps[0:64, 512 * c:512 * c + 512]))
                                else:
                                    ob = osi[0] % 2
                                    osi[0] += 1
                                    if osi[0] > 2:
                                        VE.wait_ge(ost_s[ob].h, odma[osi[0] - 3])
                                    vop(VE.tensor_copy(
                                        out=ostage[ob][0:64, :],
                                        in_=PV_ps[0:64, 512 * c:512 * c + 512]))
                                    myv = vop(VE.tensor_copy(
                                        out=ostage[ob][0:64, :],
                                        in_=PV_ps[0:64, 512 * c:512 * c + 512]))
                                    SY.wait_ge(sv.h, myv)
                                    odma[osi[0] - 1] = ost_s[ob].dma(SY.dma_start(
                                        out=O[64:128, cols], in_=ostage[ob][0:64, :]))
                                db = dsi[0] % 2
                                dsi[0] += 1
                                if dsi[0] > 2:
                                    VE.wait_ge(dst_s[db].h, ddma[dsi[0] - 3])
                                vop(VE.tensor_copy(
                                    out=dstage[db][64:65, :],
                                    in_=PV_ps[64:65, 512 * c:512 * c + 512]))
                                myv = vop(VE.tensor_copy(
                                    out=dstage[db][64:65, :],
                                    in_=PV_ps[64:65, 512 * c:512 * c + 512]))
                                SY.wait_ge(sv.h, myv)
                                p = 16 * c + h
                                ddma[dsi[0] - 1] = dst_s[db].dma(SY.dma_start(
                                    out=dens[p:p + 1, :], in_=dstage[db][64:65, :]))
                                oddone[h] = sv.v

                            for h in range(H):
                                qtile = h // 2
                                poff = 64 * (h % 2)
                                if h >= 1 and h + 1 < H:
                                    VE.wait_ge(st.h, qkdone[(h - 1, 15, 1)])
                                    stage_qh(h + 1)
                                for j in range(16):
                                    cs = list(range(j // 4, 4))
                                    nh = (len(cs) + 1) // 2
                                    lo, hi = cs[:nh], cs[nh:]
                                    off0 = 128 * (j % 4)
                                    for half, chs in ((0, lo), (1, hi)):
                                        if not chs:
                                            qkdone[(h, j, 1)] = qkdone[(h, j, 0)]
                                            continue
                                        key = (h, j - 1, half)
                                        if key in expdone:
                                            TE.wait_ge(ss.h, expdone[key])
                                        elif h > 0:
                                            TE.wait_ge(ss.h, expdone.get(
                                                (h - 1, 15, half),
                                                expdone[(h - 1, 15, 0)]))
                                        elif j == 0 and half == 0:
                                            TE.wait_ge(sv.h, vones)
                                        if j == 0 and half == 0:
                                            TE.wait_ge(sv.h, qhdone[h])
                                        mi = None
                                        for k_, c in enumerate(chs):
                                            a = max(512 * c, 128 * j)
                                            b = 512 * c + 512
                                            o = 512 * k_ + (off0 if c == cs[0] else 0)
                                            mi = TE.matmul(
                                                out=S_half[half][:, o:o + b - a],
                                                lhsT=K[:, T * qtile + 128 * j:
                                                       T * qtile + 128 * j + 128],
                                                rhs=qh[h % 2][:, a:b],
                                                start=True, stop=True)
                                        qkdone[(h, j, half)] = st.inc(mi)
                                    if j > 0:
                                        jj = j - 1
                                        TE.wait_ge(sv.h, mkdone[(h, jj)])
                                        TE.wait_ge(ss.h, expdone.get(
                                            (h, jj, 1), expdone[(h, jj, 0)]))
                                        if jj == 0 and h > 0:
                                            TE.wait_ge(sv.h, oddone[h - 1])
                                        emit_pv(h, jj)
                                        if jj % 4 == 3:
                                            emit_chunk_epilogue(h, jj // 4, qtile, poff)
                                    szA = 512 * len(lo) - off0
                                    SE.wait_ge(st.h, qkdone[(h, j, 0)])
                                    if h == 0 and j == 0:
                                        SE.wait_ge(sv.h, ebready)
                                    if (h, j - 2, 1) in pvdone or (h, j - 2) in pvdone:
                                        SE.wait_ge(st.h, pvdone[(h, j - 2)])
                                    elif h > 0 and j < 2:
                                        SE.wait_ge(st.h, pvdone[(h - 1, 14 + j)])
                                    ai = SE.activation(out=P[j % 2][:, 0:szA],
                                                       in_=S_half[0][:, off0:off0 + szA],
                                                       func=AF.Exp, bias=ebias[:],
                                                       scale=0.125)
                                    expdone[(h, j, 0)] = ss.inc(ai)
                                    if hi:
                                        szB = 512 * len(hi)
                                        SE.wait_ge(st.h, qkdone[(h, j, 1)])
                                        ai = SE.activation(
                                            out=P[j % 2][:, szA:szA + szB],
                                            in_=S_half[1][:, 0:szB],
                                            func=AF.Exp, bias=ebias[:], scale=0.125)
                                        expdone[(h, j, 1)] = ss.inc(ai)
                                    VE.wait_ge(ss.h, expdone[(h, j, 0)])
                                    m1 = sv.inc(VE.tensor_tensor(
                                        out=P[j % 2][:, 0:128],
                                        in0=P[j % 2][:, 0:128],
                                        in1=tri[:], op=ALU.mult))
                                    VE.wait_ge(sv.h, m1)
                                    mk = VE.tensor_tensor(out=P[j % 2][:, 0:128],
                                                          in0=P[j % 2][:, 0:128],
                                                          in1=tri[:], op=ALU.mult)
                                    mkdone[(h, j)] = vop(mk)
                                TE.wait_ge(sv.h, mkdone[(h, 15)])
                                emit_pv(h, 15)
                                emit_chunk_epilogue(h, 3, qtile, poff)

                        pre_div = dict(st=st.v, sv=sv.v, ss=ss.v)

                        VE.wait_ge(dst_s[0].h, dst_s[0].v)
                        VE.wait_ge(dst_s[1].h, dst_s[1].v)
                        VE.wait_ge(ost_s[0].h, ost_s[0].v)
                        VE.wait_ge(ost_s[1].h, ost_s[1].v)
                        vop(VE.reciprocal(out=recf[:], in_=dens[:]))
                        vop(VE.tensor_copy(out=hib[0:64, :], in_=recf[:]))
                        vop(VE.tensor_tensor(out=recf[:], in0=recf[:],
                                             in1=hib[0:64, :], op=ALU.subtract))
                        lo_ready = vop(VE.tensor_copy(out=lob[0:64, :], in_=recf[:]))

                        divdone = [0] * 4
                        with ExitStack() as bc:
                            bc_ps = [bc.enter_context(
                                nc.psum_tensor(f"bc{s}{i}", [128, 512], F32))
                                for i in range(2)]
                            dvd = {}
                            idx = 0
                            for c in range(4):
                                for h in range(H):
                                    p = 16 * c + h
                                    poff = 64 * (h % 2)
                                    qtile = h // 2
                                    TE.wait_ge(sv.h,
                                               lo_ready if idx < 2 else dvd[idx - 2])
                                    TE.matmul(out=bc_ps[idx % 2][poff:poff + 64, :],
                                              lhsT=selb[:, 64 * p:64 * p + 64],
                                              rhs=hib[:, :],
                                              start=True, stop=False)
                                    mi = TE.matmul(out=bc_ps[idx % 2][poff:poff + 64, :],
                                                   lhsT=selb[:, 64 * p:64 * p + 64],
                                                   rhs=lob[:, :],
                                                   start=False, stop=True)
                                    bcd = st.inc(mi)
                                    VE.wait_ge(st.h, bcd)
                                    cols = slice(T * qtile + 512 * c,
                                                 T * qtile + 512 * c + 512)
                                    ti = VE.tensor_tensor(
                                        out=O[poff:poff + 64, cols],
                                        in0=O[poff:poff + 64, cols],
                                        in1=bc_ps[idx % 2][poff:poff + 64, :],
                                        op=ALU.mult)
                                    dvd[idx] = vop(ti)
                                    idx += 1
                                divdone[c] = sv.v
                        at_end = dict(st=st.v, sv=sv.v, ss=ss.v)

                with ExitStack() as ff:
                    fslots = [ff.enter_context(
                        nc.sbuf_tensor(f"fs{s}{i}", [128, 8 * D], BF16))
                        for i in range(4)]
                    xstage2 = [ff.enter_context(
                        nc.sbuf_tensor(f"x2{s}{i}", [128, D], F32)) for i in range(2)]
                    x1span = [ff.enter_context(
                        nc.sbuf_tensor(f"x1{s}{i}", [128, D], F32)) for i in range(4)]
                    ybuf = [ff.enter_context(
                        nc.sbuf_tensor(f"yb{s}{i}", [128, D], F32)) for i in range(2)]
                    x1f = ff.enter_context(
                        nc.sbuf_tensor(f"xo{s}", [128, NCH * 513], F32))
                    x1f_r = x1f[:].rearrange("p (c w) -> p c w", w=513)
                    xr = ff.enter_context(
                        nc.sbuf_tensor(f"xr{s}", [128, NCH * SPW], BF16))
                    xk = ff.enter_context(
                        nc.sbuf_tensor(f"xk{s}", [128, NCH * SPW], BF16))
                    tmp = ff.enter_context(nc.sbuf_tensor(f"tm{s}", [128, SPW], F32))
                    kact = ff.enter_context(
                        nc.sbuf_tensor(f"ka{s}", [128, NCH * SPW], BF16))
                    krelu = [ff.enter_context(
                        nc.sbuf_tensor(f"kr{s}{i}", [128, SPW], F32)) for i in range(2)]
                    rbuf = [ff.enter_context(
                        nc.sbuf_tensor(f"rb{s}{i}", [128, D], F32)) for i in range(2)]
                    scr2 = [ff.enter_context(
                        nc.sbuf_tensor(f"sc{s}{i}", [128, D], F32)) for i in range(2)]
                    lnS = ff.enter_context(nc.sbuf_tensor(f"lS{s}", [128, 4], F32))
                    lnQ = ff.enter_context(nc.sbuf_tensor(f"lQ{s}", [128, 4], F32))
                    lnM = ff.enter_context(nc.sbuf_tensor(f"lM{s}", [128, 4], F32))
                    lnW = ff.enter_context(nc.sbuf_tensor(f"lW{s}", [128, 4], F32))
                    lnV = ff.enter_context(nc.sbuf_tensor(f"lV{s}", [128, 4], F32))
                    rstd = ff.enter_context(nc.sbuf_tensor(f"lR{s}", [128, 4], F32))
                    lnNB = ff.enter_context(nc.sbuf_tensor(f"lB{s}", [128, 4], F32))
                    murs = ff.enter_context(nc.sbuf_tensor(f"mr{s}", [128, 8], F32))
                    muks = ff.enter_context(nc.sbuf_tensor(f"mu{s}", [128, 8], F32))
                    dum_ps = ff.enter_context(
                        nc.psum_tensor(f"du{s}", [128, 512], F32))

                    def te_bridge(n):
                        for _ in range(n):
                            TE.matmul(out=dum_ps[:], lhsT=fslots[0][:, 0:128],
                                      rhs=fslots[0][:, 0:512],
                                      start=True, stop=True)

                    GP.wait_ge(st.h, pre_div["st"])
                    GP.wait_ge(sv.h, pre_div["sv"])
                    GP.wait_ge(ss.h, pre_div["ss"])
                    fsl = [wsl_s[i].dma(GP.dma_start(out=fslots[i][:],
                                                     in_=wscr[3 + i]))
                           for i in range(4)]
                    SY.wait_ge(st.h, at_end["st"])
                    SY.wait_ge(sv.h, at_end["sv"])
                    cst.dma(SY.dma_start(out=murs[:], in_=mur_h))
                    muld = cst.dma(SY.dma_start(out=muks[:], in_=muk_h))
                    VE.wait_ge(cst.h, muld)


                    def vsync(instr):
                        return vop(instr)
                        v = sv.inc(instr)
                        VE.wait_ge(sv.h, v)

                    def ln_batch():
                        vsync(VE.tensor_scalar(out=lnM[:], in0=lnS[:], scalar1=INVD,
                                               scalar2=None, op0=ALU.mult))
                        vsync(VE.tensor_tensor(out=lnW[:], in0=lnM[:], in1=lnS[:],
                                               op=ALU.mult))
                        vsync(VE.tensor_tensor(out=lnV[:], in0=lnQ[:], in1=lnW[:],
                                               op=ALU.subtract))
                        vop(VE.tensor_scalar(out=lnV[:], in0=lnV[:],
                                             scalar1=INVD, scalar2=None,
                                             op0=ALU.mult))
                        vop(VE.tensor_scalar(out=lnV[:], in0=lnV[:], scalar1=EPS,
                                             scalar2=None, op0=ALU.add))
                        vop(VE.tensor_scalar(out=rstd[:], in0=lnV[:], scalar1=0.5,
                                             scalar2=0.5, op0=ALU.mult, op1=ALU.add))
                        for _ in range(5):
                            vop(VE.reciprocal(out=lnW[:], in_=rstd[:]))
                            vop(VE.tensor_tensor(out=lnW[:], in0=lnV[:], in1=lnW[:],
                                                 op=ALU.mult))
                            vop(VE.tensor_tensor(out=lnW[:], in0=rstd[:], in1=lnW[:],
                                                 op=ALU.add))
                            vop(VE.tensor_scalar(out=rstd[:], in0=lnW[:], scalar1=0.5,
                                                 scalar2=None, op0=ALU.mult))
                        vop(VE.reciprocal(out=rstd[:], in_=rstd[:]))
                        vop(VE.scalar_tensor_tensor(out=lnNB[:], in0=lnM[:],
                                                    scalar=-1.0, in1=rstd[:],
                                                    op0=ALU.mult, op1=ALU.mult))
                        return sv.v

                    z1done = {}
                    x1fs = {}
                    sq1st = {}
                    sq2st = {}
                    x1done = {}
                    tpfdone = {}
                    x1fdone = {}
                    mixdone = {}
                    ktdone = {}
                    reludone = {}
                    sqdone = {}
                    sgdone = {}
                    gdone = {}
                    ydma = {}
                    xrld = {}
                    a2done = {}
                    yv2a = {}

                    for sp in range(NSP):
                        with ExitStack() as ps1:
                            o_ps = [ps1.enter_context(
                                nc.psum_tensor(f"op{s}{sp}{i}", [128, D], F32))
                                for i in range(2)]
                            tpf = [ps1.enter_context(
                                nc.psum_tensor(f"tf{s}{sp}{i}", [128, 512], F32))
                                for i in range(2)]
                            for u in range(4):
                                i = 4 * sp + u
                                t0 = 128 * i
                                if i >= 2:
                                    GP.wait_ge(sv.h, z1done[i - 2])
                                xrld[i] = xg[i % 2].dma(GP.dma_start(
                                    out=xstage2[i % 2][:],
                                    in_=x_h[s, t0:t0 + 128, :]))
                                TE.wait_ge(sv.h, divdone[3 if sp == 0 else sp])
                                TE.wait_ge(wsl_s[0].h, fsl[0])
                                if u == 0 and sp > 0:
                                    TE.wait_ge(sv.h, gdone[4 * sp - 1])
                                    TE.wait_ge(ss.h, sgdone[4 * sp - 1])
                                if u >= 2:
                                    TE.wait_ge(sv.h, z1done[i - 2])
                                mi = None
                                for ch in range(NCH):
                                    for half in range(2):
                                        mi = TE.matmul(
                                            out=o_ps[u % 2][:, 512 * half:512 * half + 512],
                                            lhsT=O[:, T * ch + t0:T * ch + t0 + 128],
                                            rhs=fslots[0][:, D * ch + 512 * half:D * ch + 512 * half + 512],
                                            start=(ch == 0), stop=(ch == NCH - 1))
                                omm = st.inc(mi)
                                VE.wait_ge(st.h, omm)
                                VE.wait_ge(xg[i % 2].h, xrld[i])
                                if sp > 0:
                                    VE.wait_ge(ss.h, a2done[4 * (sp - 1) + u])
                                ti = VE.tensor_tensor(out=x1span[u][:],
                                                      in0=o_ps[u % 2][:],
                                                      in1=xstage2[i % 2][:], op=ALU.add)
                                z1done[i] = vop(ti)
                                SE.wait_ge(sv.h, z1done[i])
                                if u >= 2:
                                    SE.wait_ge(ss.h, sq1st[i - 2])
                                if sp > 0:
                                    SE.wait_ge(ss.h,
                                               yv2a[4 * (sp - 1) + 2 + (u % 2)])
                                q1 = ss.inc(SE.activation(
                                    out=scr2[u % 2][:],
                                    in_=x1span[u][:], func=AF.Square,
                                    accum_out=lnQ[:, u:u + 1]))
                                SE.wait_ge(ss.h, q1)
                                sq = SE.activation(out=scr2[u % 2][:],
                                                   in_=x1span[u][:],
                                                   func=AF.Identity,
                                                   accum_out=lnS[:, u:u + 1])
                                sq1st[i] = ss.inc(sq)
                            te_bridge(64)
                            VE.wait_ge(ss.h, sq1st[4 * sp + 3])
                            ln1r = ln_batch()
                            for u in range(4):
                                i = 4 * sp + u
                                SE.wait_ge(sv.h, ln1r)
                                ai = SE.activation(out=x1span[u][:],
                                                   in_=x1span[u][:],
                                                   func=AF.Identity,
                                                   bias=lnNB[:, u:u + 1],
                                                   scale=rstd[:, u:u + 1])
                                x1done[i] = ss.inc(ai)
                                if u == 0:
                                    if sp == 0:
                                        vsync(VE.memset(x1f_r[:, :, 0:1], 0.0))
                                    else:
                                        ec = VE.tensor_copy(out=x1f_r[:, :, 0:1],
                                                            in_=x1f_r[:, :, 512:513])
                                        vsync(ec)
                                TE.wait_ge(ss.h, x1done[i])
                                if u >= 1:
                                    TE.wait_ge(sv.h, x1fdone[i - 1])
                                    TE.wait_ge(ss.h, x1fs[i - 1])
                                for b in range(8):
                                    ti = TE.transpose(
                                        out=tpf[b // 4][:, 128 * (b % 4):128 * (b % 4) + 128],
                                        in_=x1span[u][:, 128 * b:128 * b + 128],
                                        identity=idf[:])
                                tpfdone[i] = st.inc(ti)
                                SE.wait_ge(st.h, tpfdone[i])
                                sc1 = SE.copy(
                                    out=x1f_r[:, 0:4, 1 + 128 * u:1 + 128 * u + 128],
                                    in_=tpf[0][:].rearrange("p (c w) -> p c w", w=128))
                                x1fs[i] = ss.inc(sc1)
                                VE.wait_ge(st.h, tpfdone[i])
                                ev = VE.tensor_copy(
                                    out=x1f_r[:, 4:8, 1 + 128 * u:1 + 128 * u + 128],
                                    in_=tpf[1][:].rearrange("p (c w) -> p c w", w=128))
                                x1fdone[i] = vop(ev)
                            VE.wait_ge(ss.h, x1fs[4 * sp + 3])
                            for ch in range(NCH):
                                if ch >= 1:
                                    VE.wait_ge(sv.h, mixdone[(sp, ch - 1)])
                                vsync(VE.tensor_tensor(out=tmp[:],
                                                       in0=x1f_r[:, ch, 1:513],
                                                       in1=x1f_r[:, ch, 0:512],
                                                       op=ALU.subtract))
                                sv.inc(VE.scalar_tensor_tensor(
                                    out=xr[:, SPW * ch:SPW * ch + SPW], in0=tmp[:],
                                    scalar=murs[:, ch:ch + 1],
                                    in1=x1f_r[:, ch, 0:512],
                                    op0=ALU.mult, op1=ALU.add))
                                mk = VE.scalar_tensor_tensor(
                                    out=xk[:, SPW * ch:SPW * ch + SPW], in0=tmp[:],
                                    scalar=muks[:, ch:ch + 1],
                                    in1=x1f_r[:, ch, 0:512],
                                    op0=ALU.mult, op1=ALU.add)
                                mixdone[(sp, ch)] = sv.inc(mk)
                            te_bridge(40)
                        with ExitStack() as ps2:
                            k_ps = [ps2.enter_context(
                                nc.psum_tensor(f"kp{s}{sp}{i}", [128, 512], F32))
                                for i in range(2)]
                            r_ps = ps2.enter_context(
                                nc.psum_tensor(f"rp{s}{sp}", [128, D], F32))
                            v2_ps = ps2.enter_context(
                                nc.psum_tensor(f"v2{s}{sp}", [128, D], F32))
                            for kch in range(NCH):
                                TE.wait_ge(sv.h, mixdone[(sp, NCH - 1)])
                                TE.wait_ge(wsl_s[2].h, fsl[2])
                                if kch >= 2:
                                    TE.wait_ge(sv.h, sqdone[(sp, kch - 2)])
                                mi = None
                                for cch in range(NCH):
                                    mi = TE.matmul(
                                        out=k_ps[kch % 2][:],
                                        lhsT=fslots[2][:, D * cch + 128 * kch:D * cch + 128 * kch + 128],
                                        rhs=xk[:, SPW * cch:SPW * cch + SPW],
                                        start=(cch == 0), stop=(cch == NCH - 1))
                                ktdone[(sp, kch)] = st.inc(mi)
                                SE.wait_ge(st.h, ktdone[(sp, kch)])
                                if kch >= 2:
                                    SE.wait_ge(sv.h, sqdone[(sp, kch - 2)])
                                ri = SE.activation(out=krelu[kch % 2][:],
                                                   in_=k_ps[kch % 2][:], func=AF.Relu)
                                reludone[(sp, kch)] = ss.inc(ri)
                                VE.wait_ge(ss.h, reludone[(sp, kch)])
                                sq = VE.scalar_tensor_tensor(
                                    out=kact[:, SPW * kch:SPW * kch + SPW],
                                    in0=krelu[kch % 2][:], scalar=0.5,
                                    in1=k_ps[kch % 2][:],
                                    op0=ALU.mult, op1=ALU.mult)
                                sqdone[(sp, kch)] = sv.inc(sq)
                            for u in range(4):
                                i = 4 * sp + u
                                t0 = 128 * i
                                TE.wait_ge(wsl_s[1].h, fsl[1])
                                if u >= 1:
                                    TE.wait_ge(ss.h, sgdone[i - 1])
                                mi = None
                                for cch in range(NCH):
                                    for half in range(2):
                                        mi = TE.matmul(
                                            out=r_ps[:, 512 * half:512 * half + 512],
                                            lhsT=xr[:, SPW * cch + 128 * u:SPW * cch + 128 * u + 128],
                                            rhs=fslots[1][:, D * cch + 512 * half:D * cch + 512 * half + 512],
                                            start=(cch == 0), stop=(cch == NCH - 1))
                                rmm = st.inc(mi)
                                SE.wait_ge(st.h, rmm)
                                if u >= 2:
                                    SE.wait_ge(sv.h, gdone[i - 2])
                                si = SE.activation(out=rbuf[u % 2][:], in_=r_ps[:],
                                                   func=AF.Tanh, scale=0.5)
                                sgdone[i] = ss.inc(si)
                                TE.wait_ge(sv.h, sqdone[(sp, NCH - 1)])
                                TE.wait_ge(wsl_s[3].h, fsl[3])
                                if u >= 1:
                                    TE.wait_ge(sv.h, gdone[i - 1])
                                mi = None
                                for cch in range(NCH):
                                    for half in range(2):
                                        mi = TE.matmul(
                                            out=v2_ps[:, 512 * half:512 * half + 512],
                                            lhsT=kact[:, SPW * cch + 128 * u:SPW * cch + 128 * u + 128],
                                            rhs=fslots[3][:, D * cch + 512 * half:D * cch + 512 * half + 512],
                                            start=(cch == 0), stop=(cch == NCH - 1))
                                v2mm = st.inc(mi)
                                VE.wait_ge(st.h, v2mm)
                                VE.wait_ge(ss.h, sgdone[i])
                                VE.wait_ge(ss.h, sq1st[i])
                                vsync(VE.tensor_tensor(out=scr2[u % 2][:],
                                                       in0=rbuf[u % 2][:],
                                                       in1=v2_ps[:], op=ALU.mult))
                                vsync(VE.tensor_tensor(out=scr2[u % 2][:],
                                                       in0=scr2[u % 2][:],
                                                       in1=v2_ps[:], op=ALU.add))
                                gi = VE.tensor_tensor(out=x1span[u][:],
                                                      in0=scr2[u % 2][:],
                                                      in1=x1span[u][:], op=ALU.add)
                                gdone[i] = vop(gi)
                                SE.wait_ge(sv.h, gdone[i])
                                if u >= 2:
                                    SE.wait_ge(ss.h, sq2st[i - 2])
                                q1 = ss.inc(SE.activation(
                                    out=scr2[u % 2][:],
                                    in_=x1span[u][:], func=AF.Square,
                                    accum_out=lnQ[:, u:u + 1]))
                                SE.wait_ge(ss.h, q1)
                                sq = SE.activation(out=scr2[u % 2][:],
                                                   in_=x1span[u][:],
                                                   func=AF.Identity,
                                                   accum_out=lnS[:, u:u + 1])
                                sq2st[i] = ss.inc(sq)
                            VE.wait_ge(ss.h, sq2st[4 * sp + 3])
                            ln2r = ln_batch()
                            for u in range(4):
                                i = 4 * sp + u
                                t0 = 128 * i
                                SE.wait_ge(sv.h, ln2r)
                                if i >= 2:
                                    SE.wait_ge(ysem[(u + 2) % 4].h, ydma[i - 2])
                                elif s > 0:
                                    SE.wait_ge(ysem[u + 2].h,
                                               ydma_prev[14 + u])
                                a2 = ss.inc(SE.activation(
                                    out=ybuf[u % 2][:], in_=x1span[u][:],
                                    func=AF.Identity,
                                    bias=lnNB[:, u:u + 1],
                                    scale=rstd[:, u:u + 1]))
                                a2done[i] = a2
                                SE.wait_ge(ss.h, a2)
                                sp2 = SE.copy(out=scr2[u % 2][:],
                                              in_=ybuf[u % 2][:])
                                yv2 = ss.inc(sp2)
                                yv2a[i] = yv2
                                SY.wait_ge(ss.h, yv2)
                                ydma[i] = ysem[u].dma(SY.dma_start(
                                    out=y_h[s, t0:t0 + 128, :], in_=ybuf[u % 2][:]))
                    te_bridge(144)
                    ydma_prev = dict(ydma)
                    seq_end = dict(st=st.v, sv=sv.v, ss=ss.v)

        for yy in ysem:
            SY.wait_ge(yy.h, yy.v)
            TE.wait_ge(yy.h, yy.v)
        for c_ in (cst, xa[0], xa[1], dst_s[0], dst_s[1], ost_s[0], ost_s[1]):
            SY.wait_ge(c_.h, c_.v)
        for c_ in (wst_s[0], wst_s[1], wsl_s[0], wsl_s[1], wsl_s[2], wsl_s[3],
                   wscr_s, xg[0], xg[1]):
            GP.wait_ge(c_.h, c_.v)
    return nc


_NC_CACHE = {}


def _get_nc():
    if "nc" not in _NC_CACHE:
        _NC_CACHE["nc"] = build_nc()
    return _NC_CACHE["nc"]


def kernel(L0_attn_in_features, L0_attn_q_proj_weight, L0_attn_k_proj_weight,
           L0_attn_v_proj_weight, L0_attn_o_proj_weight, L0_ffn_W_r,
           L0_ffn_W_k, L0_ffn_W_v, L0_ffn_mu_r, L0_ffn_mu_k):
    x = np.ascontiguousarray(np.asarray(L0_attn_in_features, dtype=np.float32))
    ws = {
        "wq": np.ascontiguousarray(np.asarray(L0_attn_q_proj_weight, np.float32)),
        "wk": np.ascontiguousarray(np.asarray(L0_attn_k_proj_weight, np.float32)),
        "wv": np.ascontiguousarray(np.asarray(L0_attn_v_proj_weight, np.float32)),
        "wo": np.ascontiguousarray(np.asarray(L0_attn_o_proj_weight, np.float32)),
        "wr": np.ascontiguousarray(np.asarray(L0_ffn_W_r, np.float32)),
        "wkf": np.ascontiguousarray(np.asarray(L0_ffn_W_k, np.float32)),
        "wvf": np.ascontiguousarray(np.asarray(L0_ffn_W_v, np.float32)),
    }
    mur = np.ascontiguousarray(np.asarray(L0_ffn_mu_r, np.float32).reshape(8, 128).T)
    muk = np.ascontiguousarray(np.asarray(L0_ffn_mu_k, np.float32).reshape(8, 128).T)
    consts = {
        "mur": mur, "muk": muk,
        "idf": np.eye(128, dtype=np.float32),
        "tri": np.triu(np.ones((128, 128), dtype=np.float32)).astype(ml_dtypes.bfloat16),
        "sel": _sel_const(),
    }
    nc = _get_nc()
    in_maps = []
    for i in range(8):
        m = {"x": x[2 * i:2 * i + 2]}
        m.update(ws)
        m.update(consts)
        in_maps.append(m)
    res = run_bass_kernel_spmd(nc, in_maps, core_ids=list(range(8)))
    return np.concatenate([np.asarray(res.results[i]["y"]) for i in range(8)],
                          axis=0)
